# version 02
# baseline (speedup 1.0000x reference)
import contextlib
import numpy as np
import concourse.bass as bass
import concourse.mybir as mybir
from concourse.alu_op_type import AluOpType as OP
from concourse.bass_utils import run_bass_kernel_spmd

F32 = mybir.dt.float32
BF16 = mybir.dt.bfloat16
AF = mybir.ActivationFunctionType
AX = mybir.AxisListType

D = 1024
NCT = 8
EPS = 1e-5


class Prog:
    def __init__(self, nc, block, sem_names, es):
        self.nc = nc
        self.block = block
        self.sem = {n: es.enter_context(nc.semaphore(n)) for n in sem_names}
        self.cnt = {n: 0 for n in sem_names}

    def inc(self, ret, name, n=1):
        ret.then_inc(self.sem[name], n)
        self.cnt[name] += n

    def wait(self, eng, name):
        eng.wait_ge(self.sem[name], self.cnt[name])


def ln_stats(p, ones_src, ones_sq, src_tiles, sq8, ps_r0, ps_r1, negmu, inv, scr_row,
             nct=NCT, T=2048):
    block = p.block
    ntch = T // 512
    for tch in range(ntch):
        sl = slice(tch * 512, (tch + 1) * 512)

        def _sq(v, sl=sl):
            p.wait(v, "t")
            for ct in range(nct):
                r = v.tensor_tensor(out=sq8[:, ct, :], in0=src_tiles(ct, sl),
                                    in1=src_tiles(ct, sl), op=OP.mult)
            p.inc(r, "v")
        block.vector(_sq)

        def _mm(t, sl=sl):
            p.wait(t, "v")
            for ct in range(nct):
                t.matmul(ps_r0[:], ones_src[:], src_tiles(ct, sl),
                         start=(ct == 0), stop=(ct == nct - 1))
            for ct in range(nct):
                r = t.matmul(ps_r1[:], ones_sq[:], sq8[:, ct, :],
                             start=(ct == 0), stop=(ct == nct - 1))
            p.inc(r, "t")
        block.tensor(_mm)

        def _rows(v, sl=sl):
            p.wait(v, "t")
            v.tensor_scalar_mul(out=negmu[:, sl], in0=ps_r0[:], scalar1=-1.0 / D)
            v.tensor_tensor(out=inv[:, sl], in0=negmu[:, sl], in1=negmu[:, sl],
                            op=OP.mult)
            r = v.scalar_tensor_tensor(out=scr_row[:, sl], in0=ps_r1[:],
                                       scalar=1.0 / D, in1=inv[:, sl],
                                       op0=OP.mult, op1=OP.subtract)
            p.inc(r, "v")
        block.vector(_rows)

    def _eps(v):
        r = v.tensor_scalar_add(out=scr_row[:], in0=scr_row[:], scalar1=EPS)
        p.inc(r, "v")
    block.vector(_eps)

    def _sqrt(s):
        p.wait(s, "v")
        r = s.activation(out=inv[:], in_=scr_row[:], func=AF.Sqrt)
        p.inc(r, "s")
    block.scalar(_sqrt)

    def _rcp(v):
        p.wait(v, "s")
        r = v.reciprocal(out=inv[:], in_=inv[:])
        p.inc(r, "v")
    block.vector(_rcp)


def ln_apply(p, onesf, src_tiles, out_tiles, negmu, inv, ps_m, ps_i, t1,
             nct=NCT, T=2048, add_res=None, out_res=None):
    block = p.block
    ntch = T // 512
    for tch in range(ntch):
        sl = slice(tch * 512, (tch + 1) * 512)

        def _bc(t, sl=sl):
            p.wait(t, "v")
            t.matmul(ps_m[:], onesf[0:1, :], negmu[:, sl], start=True, stop=True)
            r = t.matmul(ps_i[:], onesf[32:33, :], inv[:, sl], start=True, stop=True)
            p.inc(r, "t")
        block.tensor(_bc)

        def _ap(v, sl=sl, tch=tch):
            p.wait(v, "t")
            for ct in range(nct):
                v.scalar_tensor_tensor(out=t1[:], in0=ps_m[:], scalar=1.0,
                                       in1=src_tiles(ct, sl), op0=OP.mult,
                                       op1=OP.add)
                if add_res is None:
                    r = v.tensor_tensor(out=out_tiles(ct, sl), in0=t1[:],
                                        in1=ps_i[:], op=OP.mult)
                else:
                    v.tensor_tensor(out=t1[:], in0=t1[:], in1=ps_i[:], op=OP.mult)
                    r = v.tensor_tensor(out=out_res(ct, sl), in0=t1[:],
                                        in1=add_res(ct, sl), op=OP.add)
            p.inc(r, "v")
        block.vector(_ap)


def build_B(dbg=False):
    nc = bass.Bass("TRN2", target_bir_lowering=False, debug=False,
                   detect_race_conditions=False)
    xa = nc.dram_tensor("xa", [D, 2048], F32, kind="ExternalInput").ap()
    gw = nc.dram_tensor("gw", [D, 8], F32, kind="ExternalInput").ap()
    e0 = nc.dram_tensor("e0", [D, D], F32, kind="ExternalInput").ap()
    e1 = nc.dram_tensor("e1", [D, D], F32, kind="ExternalInput").ap()
    ow = nc.dram_tensor("ow", [D, D], F32, kind="ExternalInput").ap()
    xb = nc.dram_tensor("xb", [D, 2048], F32, kind="ExternalOutput").ap()
    if dbg:
        dbg_o = {n: nc.dram_tensor(n, shp, F32, kind="ExternalOutput").ap()
                 for n, shp in [("d_rows", [128, 2048]), ("d_xn2", [128, 8 * 2048]),
                                ("d_lg", [8, 2048]), ("d_srow", [128, 2048]),
                                ("d_fin", [128, 8 * 2048]), ("d_slot", [128, 8 * 2048]),
                                ("d_ltok", [128, 16]), ("d_sm1", [128, 16]),
                                ("d_sm2", [128, 16]), ("d_sm3", [128, 16]),
                                ("d_sm0", [128, 16]), ("d_stok", [128, 128])]}

    es = contextlib.ExitStack()
    sbt = lambda name, shape, dt: es.enter_context(nc.sbuf_tensor(name, shape, dt))
    pst = lambda name, shape: es.enter_context(nc.psum_tensor(name, shape, F32))
    with es:
        xa_f = sbt("xa_f", [128, NCT, 2048], F32)
        xn2 = sbt("xn2", [128, NCT, 2048], BF16)
        fin = sbt("fin", [128, NCT, 2048], BF16)
        sq8 = sbt("sq8", [128, NCT, 512], BF16)
        gwb = sbt("gwb", [128, NCT, 8], BF16)
        e0b = sbt("e0b", [128, NCT, 1024], BF16)
        e1b = sbt("e1b", [128, NCT, 1024], BF16)
        owb = sbt("owb", [128, NCT, 1024], BF16)
        onesb = sbt("onesb", [128, 1], BF16)
        onesfb = sbt("onesfb", [128, 128], BF16)
        onescf = sbt("onescf", [128, 1], F32)
        onesf = sbt("onesf", [128, 128], F32)
        rows_t = sbt("rows_t", [128, 2048], F32)
        negmu, inv, scr_row = rows_t[0:1], rows_t[32:33], rows_t[64:65]
        negmu2, inv2 = negmu, inv
        lg16 = sbt("lg16", [128, 2048], BF16)
        ltok = sbt("ltok", [128, 16, 16], BF16)
        stok = sbt("stok", [128, 16, 128], BF16)
        srow = lg16
        s0r = sbt("s0r", [128, 512], BF16)
        s1r = sbt("s1r", [128, 512], BF16)
        fb0 = sbt("fb0", [128, 512], BF16)
        t1 = sbt("t1", [128, 512], F32)
        sm = [sbt(f"sm{i}", [128, 16], F32) for i in range(6)]
        slot = xn2

        ps_a = pst("ps_a", [128, 512])
        ps_b = pst("ps_b", [128, 512])
        ps_c = pst("ps_c", [128, 512])
        ps_d = pst("ps_d", [128, 512])
        ps_e = pst("ps_e", [128, 512])
        ps_f = pst("ps_f", [128, 512])
        ps_r0 = pst("ps_r0", [1, 512])
        ps_r1 = pst("ps_r1", [1, 512])

        with nc.Block() as block:
            p = Prog(nc, block, ["ld", "gld", "sld", "t", "v", "s", "st", "vv"], es)

            def _lds(sync):
                for ct in range(NCT):
                    p.inc(sync.dma_start(out=xa_f[:, ct, :],
                                         in_=xa[ct * 128:(ct + 1) * 128, :]),
                          "ld", 16)
            block.sync(_lds)

            def _ldg(g):
                for src, dst in ((gw, gwb), (e0, e0b), (e1, e1b), (ow, owb)):
                    for ct in range(NCT):
                        p.inc(g.dma_start(out=dst[:, ct, :],
                                          in_=src[ct * 128:(ct + 1) * 128, :]),
                              "gld", 16)
            block.gpsimd(_ldg)

            def _init(v):
                v.memset(onesb[:], 1.0)
                v.memset(onesfb[:], 1.0)
                v.memset(onescf[:], 1.0)
                v.memset(onesf[:], 1.0)
                v.memset(stok[:], 0.0)
                r = v.memset(lg16[0:16, :], 0.0)
                p.inc(r, "v")
                p.wait(v, "ld")
                p.wait(v, "gld")
                r = v.engine_nop()
                p.inc(r, "v")
            block.vector(_init)

            ln_stats(p, onescf, onesb, lambda ct, sl: xa_f[:, ct, sl], sq8,
                     ps_r0, ps_r1, negmu, inv, scr_row)
            ln_apply(p, onesf, lambda ct, sl: xa_f[:, ct, sl],
                     lambda ct, sl: xn2[:, ct, sl], negmu, inv, ps_a, ps_b, t1)

            if dbg:
                def _d1(g):
                    p.wait(g, "v")
                    p.inc(g.dma_start(out=dbg_o["d_rows"][:], in_=rows_t[:]), "st", 16)
                    p.inc(g.dma_start(out=dbg_o["d_xn2"][:],
                                      in_=xn2[:].rearrange("p a b -> p (a b)")), "st", 16)
                block.gpsimd(_d1)

            for tch in range(4):
                sl = slice(tch * 512, (tch + 1) * 512)

                def _g(t, sl=sl):
                    p.wait(t, "v")
                    for ct in range(NCT):
                        r = t.matmul(ps_c[:8, 0:512], gwb[:, ct, :],
                                     xn2[:, ct, sl], start=(ct == 0),
                                     stop=(ct == NCT - 1))
                    p.inc(r, "t")
                block.tensor(_g)

                def _e(v, sl=sl):
                    p.wait(v, "t")
                    r = v.tensor_copy(out=lg16[0:8, sl], in_=ps_c[0:8, 0:512])
                    p.inc(r, "v")
                block.vector(_e)

            if dbg:
                def _d2(g):
                    p.wait(g, "v")
                    p.inc(g.dma_start(out=dbg_o["d_lg"][:], in_=lg16[0:8, :]), "st", 16)
                block.gpsimd(_d2)

            def _tr(s):
                p.wait(s, "v")
                for i in range(16):
                    p.inc(s.dma_start(out=ltok[:, i, :],
                                      in_=lg16[0:16, i * 128:(i + 1) * 128],
                                      transpose=True), "sld", 16)
            block.scalar(_tr)

            for i in range(16):
                def _sm1(v, i=i):
                    if i == 0:
                        p.wait(v, "sld")
                    p.wait(v, "s")
                    rr = v.tensor_reduce(out=sm[0][:, 0:1], in_=ltok[:, i, 0:8],
                                    axis=AX.X, op=OP.max)
                    p.inc(rr, "vv")
                    p.wait(v, "vv")
                    r = v.tensor_scalar_mul(out=sm[0][:, 1:2], in0=sm[0][:, 0:1],
                                            scalar1=-1.0)
                    p.inc(r, "v")
                block.vector(_sm1)

                def _sm2(s, i=i):
                    p.wait(s, "v")
                    r = s.activation(out=sm[1][:, 0:8], in_=ltok[:, i, 0:8],
                                     func=AF.Exp, bias=sm[0][:, 1:2])
                    p.inc(r, "s")
                block.scalar(_sm2)

                def _sm3(v, i=i):
                    p.wait(v, "s")
                    rr = v.tensor_reduce(out=sm[2][:, 0:1], in_=sm[1][:, 0:8],
                                         axis=AX.X, op=OP.add)
                    p.inc(rr, "vv")
                    p.wait(v, "vv")
                    rr = v.reciprocal(out=sm[2][:, 1:2], in_=sm[2][:, 0:1])
                    p.inc(rr, "vv")
                    p.wait(v, "vv")
                    rr = v.tensor_scalar(out=sm[3][:, 0:8], in0=sm[1][:, 0:8],
                                         scalar1=sm[2][:, 1:2], scalar2=None,
                                         op0=OP.mult)
                    p.inc(rr, "vv")
                    p.wait(v, "vv")
                    rr = v.tensor_scalar(out=sm[4][:, 0:8], in0=sm[3][:, 0:8],
                                         scalar1=0.3, scalar2=None, op0=OP.is_gt)
                    p.inc(rr, "vv")
                    p.wait(v, "vv")
                    rr = v.tensor_reduce(out=sm[2][:, 2:3], in_=sm[4][:, 0:8],
                                         axis=AX.X, op=OP.add)
                    p.inc(rr, "vv")
                    p.wait(v, "vv")
                    rr = v.tensor_scalar(out=sm[2][:, 3:4], in0=sm[2][:, 2:3],
                                         scalar1=8.0, scalar2=-1.0,
                                         op0=OP.subtract, op1=OP.mult)
                    p.inc(rr, "vv")
                    p.wait(v, "vv")
                    v.tensor_tensor(out=stok[:, i, 0:1], in0=sm[2][:, 3:4],
                                    in1=sm[3][:, 0:1], op=OP.mult)
                    r = v.tensor_tensor(out=stok[:, i, 64:65], in0=sm[2][:, 2:3],
                                        in1=sm[3][:, 1:2], op=OP.mult)
                    p.inc(r, "v")
                block.vector(_sm3)

            if dbg:
                def _d5(g):
                    p.wait(g, "v")
                    p.inc(g.dma_start(out=dbg_o["d_ltok"][:], in_=ltok[:, 15, :]), "st", 16)
                    p.inc(g.dma_start(out=dbg_o["d_sm0"][:], in_=sm[0][:]), "st", 16)
                    p.inc(g.dma_start(out=dbg_o["d_sm1"][:], in_=sm[1][:]), "st", 16)
                    p.inc(g.dma_start(out=dbg_o["d_sm2"][:], in_=sm[2][:]), "st", 16)
                    p.inc(g.dma_start(out=dbg_o["d_sm3"][:], in_=sm[3][:]), "st", 16)
                    p.inc(g.dma_start(out=dbg_o["d_stok"][:], in_=stok[:, 15, :]), "st", 16)
                block.gpsimd(_d5)

            def _tr2(s):
                p.wait(s, "v")
                for i in range(16):
                    p.inc(s.dma_start(out=srow[:, i * 128:(i + 1) * 128],
                                      in_=stok[:, i, :], transpose=True),
                          "sld", 16)
            block.scalar(_tr2)

            if dbg:
                def _d3(g):
                    p.wait(g, "sld")
                    p.inc(g.dma_start(out=dbg_o["d_srow"][:], in_=srow[:]), "st", 16)
                block.gpsimd(_d3)

            par = [(ps_a, ps_b), (ps_e, ps_f)]
            for tch in range(4):
                sl = slice(tch * 512, (tch + 1) * 512)

                def _bc(t, sl=sl, tch=tch):
                    if tch == 0:
                        p.wait(t, "sld")
                    p.wait(t, "v")
                    t.matmul(ps_c[:], onesfb[0:1, :], srow[0:1, sl], start=True, stop=True)
                    r = t.matmul(ps_d[:], onesfb[64:65, :], srow[64:65, sl], start=True, stop=True)
                    p.inc(r, "t")
                block.tensor(_bc)

                def _ev(v):
                    p.wait(v, "t")
                    v.tensor_copy(out=s0r[:], in_=ps_c[:])
                    r = v.tensor_copy(out=s1r[:], in_=ps_d[:])
                    p.inc(r, "v")
                block.vector(_ev)

                for Ht in range(NCT):
                    pa, pb = par[Ht % 2]

                    def _h(t, pa=pa, pb=pb, sl=sl, Ht=Ht):
                        p.wait(t, "v")
                        for ct in range(NCT):
                            t.matmul(pa[:], e0b[:, ct, Ht * 128:(Ht + 1) * 128],
                                     xn2[:, ct, sl], start=(ct == 0),
                                     stop=(ct == NCT - 1))
                        for ct in range(NCT):
                            r = t.matmul(pb[:], e1b[:, ct, Ht * 128:(Ht + 1) * 128],
                                         xn2[:, ct, sl], start=(ct == 0),
                                         stop=(ct == NCT - 1))
                        p.inc(r, "t")
                    block.tensor(_h)

                    def _c(v, pa=pa, pb=pb, sl=sl, Ht=Ht):
                        p.wait(v, "t")
                        v.scalar_tensor_tensor(out=fb0[:], in0=pa[:], scalar=1.0,
                                               in1=s0r[:], op0=OP.mult, op1=OP.mult)
                        v.scalar_tensor_tensor(out=t1[:], in0=pb[:], scalar=1.0,
                                               in1=s1r[:], op0=OP.mult, op1=OP.mult)
                        r = v.tensor_tensor(out=fin[:, Ht, sl], in0=fb0[:],
                                            in1=t1[:], op=OP.add)
                        p.inc(r, "v")
                    block.vector(_c)

            for tch in range(4):
                sl = slice(tch * 512, (tch + 1) * 512)
                for Dt in range(NCT):
                    pa, pb = par[Dt % 2]

                    def _o(t, pa=pa, sl=sl, Dt=Dt):
                        p.wait(t, "v")
                        for ct in range(NCT):
                            r = t.matmul(pa[:], owb[:, ct, Dt * 128:(Dt + 1) * 128],
                                         fin[:, ct, sl], start=(ct == 0),
                                         stop=(ct == NCT - 1))
                        p.inc(r, "t")
                    block.tensor(_o)

                    def _e2(v, pa=pa, sl=sl, Dt=Dt):
                        p.wait(v, "t")
                        r = v.tensor_copy(out=slot[:, Dt, sl], in_=pa[:])
                        p.inc(r, "v")
                    block.vector(_e2)

            if dbg:
                def _d4(g):
                    p.wait(g, "v")
                    p.inc(g.dma_start(out=dbg_o["d_fin"][:],
                                      in_=fin[:].rearrange("p a b -> p (a b)")), "st", 16)
                    p.inc(g.dma_start(out=dbg_o["d_slot"][:],
                                      in_=slot[:].rearrange("p a b -> p (a b)")), "st", 16)
                block.gpsimd(_d4)

            ln_stats(p, onesb, onesb, lambda ct, sl: slot[:, ct, sl], sq8,
                     ps_r0, ps_r1, negmu2, inv2, scr_row)
            ln_apply(p, onesf, lambda ct, sl: slot[:, ct, sl],
                     None, negmu2, inv2, ps_c, ps_d, t1,
                     add_res=lambda ct, sl: xa_f[:, ct, sl],
                     out_res=lambda ct, sl: xa_f[:, ct, sl])

            def _st(sync):
                p.wait(sync, "v")
                for ct in range(NCT):
                    p.inc(sync.dma_start(out=xb[ct * 128:(ct + 1) * 128, :],
                                         in_=xa_f[:, ct, :]), "st", 16)
                p.wait(sync, "st")
            block.sync(_st)
    return nc


def build_A():
    nc = bass.Bass("TRN2", target_bir_lowering=False, debug=False,
                   detect_race_conditions=False)
    T = 8192
    xt = nc.dram_tensor("xt", [D, T], F32, kind="ExternalInput").ap()
    wq = nc.dram_tensor("wq", [D, 16 * 192], F32, kind="ExternalInput").ap()
    bq = nc.dram_tensor("bq", [128, 24], F32, kind="ExternalInput").ap()
    gm = nc.dram_tensor("gm", [128, 8], F32, kind="ExternalInput").ap()
    bt = nc.dram_tensor("bt", [128, 8], F32, kind="ExternalInput").ap()
    wo = nc.dram_tensor("wo", [D, D], F32, kind="ExternalInput").ap()
    bo = nc.dram_tensor("bo", [128, 8], F32, kind="ExternalInput").ap()
    xnd = nc.dram_tensor("xnd", [D, T], BF16, kind="Internal").ap()
    kmd = nc.dram_tensor("kmd", [1, 64], F32, kind="Internal").ap()
    xa_out = nc.dram_tensor("xa_out", [D, 2048], F32, kind="ExternalOutput").ap()

    es = contextlib.ExitStack()
    sbt = lambda name, shape, dt: es.enter_context(nc.sbuf_tensor(name, shape, dt))
    pst = lambda name, shape: es.enter_context(nc.psum_tensor(name, shape, F32))
    with es:
        rows = sbt("rows", [128, T], F32)
        xplain = sbt("xplain", [128, NCT, 2048], BF16)
        ctxall = sbt("ctxall", [128, NCT, 2048], BF16)
        onesb = sbt("onesb", [128, 1], BF16)
        onescf = sbt("onescf", [128, 1], F32)
        onesff = sbt("onesff", [128, 128], F32)
        gamt = sbt("gamt", [128, 8], F32)
        bett = sbt("bett", [128, 8], F32)
        bqt = sbt("bqt", [128, 24], F32)
        bot = sbt("bot", [128, 8], F32)
        kabs = sbt("kabs", [64, 1], F32)
        krow = sbt("krow", [1, 64], F32)
        ps_a = pst("ps_a", [128, 512])
        ps_b = pst("ps_b", [128, 512])
        ps_r0 = ps_a[0:1, :]
        ps_r1 = ps_b[0:1, :]
        ps_sA = pst("ps_sA", [128, 1024])
        ps_sB = pst("ps_sB", [128, 1024])
        ps_ctx = pst("ps_ctx", [128, 512])

        with nc.Block() as block:
            p = Prog(nc, block, ["ld", "gld", "sld", "t", "v", "s", "st",
                                 "vv", "g"], es)

            def _init(v):
                v.memset(onesb[:], 1.0)
                v.memset(onescf[:], 1.0)
                v.memset(onesff[:], 1.0)
                v.memset(rows[0:1, :], 0.0)
                r = v.memset(rows[32:33, :], 0.0)
                p.inc(r, "v")
            block.vector(_init)

            def _lds(g):
                p.inc(g.dma_start(out=gamt[:], in_=gm[:]), "gld", 16)
                p.inc(g.dma_start(out=bett[:], in_=bt[:]), "gld", 16)
                p.inc(g.dma_start(out=bqt[:], in_=bq[:]), "gld", 16)
                p.inc(g.dma_start(out=bot[:], in_=bo[:]), "gld", 16)
            block.gpsimd(_lds)

            es1 = contextlib.ExitStack()
            sb1 = lambda name, shape, dt: es1.enter_context(nc.sbuf_tensor(name, shape, dt))
            xs = sb1("xs", [128, 2, T], F32)
            murep = sb1("murep", [128, T], BF16)
            ivrep = sb1("ivrep", [128, T], BF16)
            xnp = sb1("xnp", [128, 2, 512], BF16)
            sqj = sb1("sqj", [128, 2, 512], BF16)
            t1 = sb1("t1", [128, 512], F32)

            for ct in range(NCT):
                def _ld(sync, ct=ct):
                    if ct >= 2:
                        p.wait(sync, "v")
                    p.inc(sync.dma_start(out=xs[:, ct % 2, :],
                                         in_=xt[ct * 128:(ct + 1) * 128, :]),
                          "ld", 16)
                block.sync(_ld)
                for tch in range(16):
                    sl = slice(tch * 512, (tch + 1) * 512)

                    def _sq(v, ct=ct, sl=sl, tch=tch):
                        if tch == 0:
                            p.wait(v, "ld")
                        p.wait(v, "t")
                        r = v.tensor_tensor(out=sqj[:, tch % 2, :],
                                            in0=xs[:, ct % 2, sl],
                                            in1=xs[:, ct % 2, sl], op=OP.mult)
                        p.inc(r, "v")
                    block.vector(_sq)

                    def _mm(t, ct=ct, sl=sl, tch=tch):
                        p.wait(t, "v")
                        t.matmul(ps_r0, onescf[:], xs[:, ct % 2, sl],
                                 start=True, stop=True)
                        r = t.matmul(ps_r1, onesb[:], sqj[:, tch % 2, :],
                                     start=True, stop=True)
                        p.inc(r, "t")
                    block.tensor(_mm)

                    def _acc(v, sl=sl):
                        p.wait(v, "t")
                        v.tensor_tensor(out=rows[0:1, sl], in0=rows[0:1, sl],
                                        in1=ps_r0, op=OP.add)
                        r = v.tensor_tensor(out=rows[32:33, sl], in0=rows[32:33, sl],
                                            in1=ps_r1, op=OP.add)
                        p.inc(r, "v")
                    block.vector(_acc)

            for tch in range(16):
                sl = slice(tch * 512, (tch + 1) * 512)

                def _fin(v, sl=sl):
                    v.tensor_scalar_mul(out=rows[0:1, sl], in0=rows[0:1, sl],
                                        scalar1=-1.0 / D)
                    v.tensor_scalar_mul(out=rows[32:33, sl], in0=rows[32:33, sl],
                                        scalar1=1.0 / D)
                    v.tensor_tensor(out=ps_r0, in0=rows[0:1, sl],
                                    in1=rows[0:1, sl], op=OP.mult)
                    v.scalar_tensor_tensor(out=rows[64:65, sl], in0=ps_r0,
                                           scalar=-1.0, in1=rows[32:33, sl],
                                           op0=OP.mult, op1=OP.add)
                    r = v.tensor_scalar_add(out=rows[64:65, sl],
                                            in0=rows[64:65, sl], scalar1=EPS)
                    p.inc(r, "v")
                block.vector(_fin)

            def _sqrt(s):
                p.wait(s, "v")
                r = s.activation(out=rows[64:65, :], in_=rows[64:65, :], func=AF.Sqrt)
                p.inc(r, "s")
            block.scalar(_sqrt)

            def _rcp(v):
                p.wait(v, "s")
                rr = v.reciprocal(out=rows[32:33, :], in_=rows[64:65, :])
                p.inc(rr, "vv")
                p.wait(v, "vv")
                r = v.engine_nop()
                p.inc(r, "v")
            block.vector(_rcp)

            for tch in range(16):
                sl = slice(tch * 512, (tch + 1) * 512)

                def _bc(t, sl=sl):
                    p.wait(t, "v")
                    t.matmul(ps_a[:], onesff[0:1, :], rows[0:1, sl],
                             start=True, stop=True)
                    r = t.matmul(ps_b[:], onesff[32:33, :], rows[32:33, sl],
                                 start=True, stop=True)
                    p.inc(r, "t")
                block.tensor(_bc)

                def _cp(v, sl=sl):
                    p.wait(v, "t")
                    v.tensor_copy(out=murep[:, sl], in_=ps_a[:])
                    r = v.tensor_copy(out=ivrep[:, sl], in_=ps_b[:])
                    p.inc(r, "v")
                block.vector(_cp)

            for ct in range(NCT):
                def _ld2(sync, ct=ct):
                    if ct >= 2:
                        p.wait(sync, "v")
                    else:
                        p.wait(sync, "t")
                    p.inc(sync.dma_start(out=xs[:, ct % 2, :],
                                         in_=xt[ct * 128:(ct + 1) * 128, :]),
                          "ld", 16)
                block.sync(_ld2)

                for tch in range(16):
                    sl = slice(tch * 512, (tch + 1) * 512)

                    def _ap(v, ct=ct, tch=tch, sl=sl):
                        if tch == 0:
                            p.wait(v, "ld")
                        p.wait(v, "g")
                        v.tensor_tensor(out=t1[:], in0=xs[:, ct % 2, sl],
                                        in1=murep[:, sl], op=OP.add)
                        v.tensor_tensor(out=t1[:], in0=t1[:], in1=ivrep[:, sl],
                                        op=OP.mult)
                        if tch < 4:
                            v.tensor_copy(out=xplain[:, ct, sl], in_=t1[:])
                        r = v.tensor_scalar(out=xnp[:, tch % 2, :], in0=t1[:],
                                            scalar1=gamt[:, ct:ct + 1],
                                            scalar2=bett[:, ct:ct + 1],
                                            op0=OP.mult, op1=OP.add)
                        p.inc(r, "v")
                    block.vector(_ap)

                    def _st1(g, ct=ct, tch=tch, sl=sl):
                        p.wait(g, "v")
                        p.inc(g.dma_start(out=xnd[ct * 128:(ct + 1) * 128, sl],
                                          in_=xnp[:, tch % 2, :]), "g", 16)
                    block.gpsimd(_st1)

            def _w1(v):
                p.wait(v, "g")
                r = v.engine_nop()
                p.inc(r, "v")
            block.vector(_w1)
            es1.close()

            es2 = contextlib.ExitStack()
            sb2 = lambda name, shape, dt: es2.enter_context(nc.sbuf_tensor(name, shape, dt))
            qkvp = sb2("qkvp", [128, 3, T], BF16)
            xnstr = sb2("xnstr", [128, 2, NCT, 512], BF16)
            wqb = sb2("wqb", [128, 1, NCT, 384], BF16)
            kvbuf = sb2("kvbuf", [80, T], BF16)
            vtok = sb2("vtok", [128, 64, 80], BF16)
            qtil = sb2("qtil", [65, 2048], BF16)
            qsq = sb2("qsq", [64, 2048], BF16)
            pbuf = sb2("pbuf", [128, 2, 1024], BF16)
            ctxe = sb2("ctxe", [64, 512], BF16)

            def _kv0(v):
                v.memset(kvbuf[64:80, :], 0.0)
                r = v.memset(kvbuf[64:65, :], 1.0)
                p.inc(r, "v")
            block.vector(_kv0)

            for hp in range(8):
                def _ldw(g, hp=hp):
                    if hp >= 1:
                        p.wait(g, "t")
                    for ct in range(NCT):
                        p.inc(g.dma_start(out=wqb[:, 0, ct, :],
                                          in_=wq[ct * 128:(ct + 1) * 128,
                                                 hp * 384:(hp + 1) * 384]),
                              "gld", 16)
                block.gpsimd(_ldw)

                for tch in range(16):
                    sl = slice(tch * 512, (tch + 1) * 512)

                    def _lx(sync, hp=hp, tch=tch, sl=sl):
                        if hp == 0 and tch < 2:
                            p.wait(sync, "g")
                        else:
                            p.wait(sync, "t")
                        r = sync.dma_start(
                            out=xnstr[:, tch % 2, :, :],
                            in_=xnd[:, sl].rearrange("(a p) b -> p a b", p=128))
                        p.inc(r, "ld", 16)
                    block.sync(_lx)

                    def _qmm(t, hp=hp, tch=tch):
                        p.wait(t, "ld")
                        p.wait(t, "v")
                        if tch == 0:
                            p.wait(t, "gld")
                        r = None
                        for c3 in range(3):
                            psx = (ps_a, ps_b, ps_ctx)[c3]
                            for ct in range(NCT):
                                r = t.matmul(psx[:, 0:512],
                                             wqb[:, 0, ct, c3 * 128:(c3 + 1) * 128],
                                             xnstr[:, tch % 2, ct, :],
                                             start=(ct == 0), stop=(ct == NCT - 1))
                        p.inc(r, "t")
                    block.tensor(_qmm)

                    def _qev(v, hp=hp, tch=tch, sl=sl):
                        p.wait(v, "t")
                        r = None
                        for c3 in range(3):
                            psx = (ps_a, ps_b, ps_ctx)[c3]
                            r = v.tensor_scalar(out=qkvp[:, c3, sl], in0=psx[:, 0:512],
                                                scalar1=bqt[:, hp * 3 + c3:hp * 3 + c3 + 1],
                                                scalar2=None, op0=OP.add)
                        p.inc(r, "v")
                    block.vector(_qev)

                for h01 in range(2):
                    vsrc = qkvp[0:64, 1, :] if h01 == 0 else qkvp[64:128, 2, :]
                    ksrc = qkvp[64:128, 0, :] if h01 == 0 else qkvp[0:64, 2, :]
                    qsrc = qkvp[0:64, 0, :] if h01 == 0 else qkvp[64:128, 1, :]

                    def _vb(v, vsrc=vsrc):
                        p.wait(v, "sld")
                        p.wait(v, "t")
                        r = v.tensor_copy(out=kvbuf[0:64, :], in_=vsrc)
                        p.inc(r, "v")
                    block.vector(_vb)

                    def _vt(s):
                        p.wait(s, "v")
                        for kt in range(64):
                            p.inc(s.dma_start(out=vtok[:, kt, :],
                                              in_=kvbuf[:, kt * 128:(kt + 1) * 128],
                                              transpose=True), "sld", 16)
                    block.scalar(_vt)

                    def _kb(v, ksrc=ksrc, qsrc=qsrc):
                        p.wait(v, "sld")
                        v.tensor_copy(out=kvbuf[0:64, :], in_=ksrc)
                        v.tensor_scalar_mul(out=qtil[0:64, :], in0=qsrc[:, 0:2048],
                                            scalar1=0.125)
                        v.tensor_tensor(out=qsq[:], in0=qsrc[:, 0:2048],
                                        in1=qsrc[:, 0:2048], op=OP.mult)
                        rr = v.tensor_reduce(out=kabs[:], in_=kvbuf[0:64, :],
                                             axis=AX.X, op=OP.max,
                                             apply_absolute_value=True)
                        p.inc(rr, "vv")
                        p.wait(v, "vv")
                        r = v.engine_nop()
                        p.inc(r, "v")
                    block.vector(_kb)

                    def _km(g):
                        p.wait(g, "v")
                        p.inc(g.dma_start(out=kmd[:].rearrange("a b -> b a"),
                                          in_=kabs[:]), "g", 16)
                    block.gpsimd(_km)

                    def _km2(sync):
                        p.wait(sync, "g")
                        p.inc(sync.dma_start(out=krow[:], in_=kmd[:]), "ld", 16)
                    block.sync(_km2)

                    def _km3(v):
                        p.wait(v, "ld")
                        rr = v.tensor_reduce(out=krow[:, 0:1], in_=krow[:],
                                             axis=AX.X, op=OP.max)
                        p.inc(rr, "vv")
                        p.wait(v, "vv")
                        r = v.engine_nop()
                        p.inc(r, "v")
                    block.vector(_km3)

                    for i4 in range(4):
                        sl = slice(i4 * 512, (i4 + 1) * 512)

                        def _qn(t, sl=sl):
                            p.wait(t, "v")
                            r = t.matmul(ps_r0, onesb[0:64, :], qsq[:, sl],
                                         start=True, stop=True)
                            p.inc(r, "t")
                        block.tensor(_qn)

                        def _qs(s, sl=sl):
                            p.wait(s, "t")
                            r = s.activation(out=rows[64:65, sl], in_=ps_r0,
                                             func=AF.Sqrt)
                            p.inc(r, "s")
                        block.scalar(_qs)

                    def _mrow(v):
                        p.wait(v, "s")
                        r = v.tensor_scalar(out=qtil[64:65, 0:2048],
                                            in0=rows[64:65, 0:2048],
                                            scalar1=krow[0:1, 0:1], scalar2=-1.0,
                                            op0=OP.mult, op1=OP.mult)
                        p.inc(r, "v")
                    block.vector(_mrow)

                    for qt in range(4):
                        qsl = slice(qt * 512, (qt + 1) * 512)
                        s_base = p.cnt["s"]
                        t_base = p.cnt["t"]
                        for kg in range(32):
                            def _sc(t, kg=kg, qsl=qsl, s_base=s_base):
                                psx = ps_sA if kg % 2 == 0 else ps_sB
                                if kg == 0:
                                    p.wait(t, "v")
                                if kg >= 2:
                                    t.wait_ge(p.sem["s"], s_base + kg - 1)
                                r = None
                                for i2 in range(2):
                                    ktile = kg * 2 + i2
                                    r = t.matmul(psx[:, i2 * 512:(i2 + 1) * 512],
                                                 kvbuf[0:65, ktile * 128:(ktile + 1) * 128],
                                                 qtil[:, qsl],
                                                 start=True, stop=True)
                                if kg >= 1:
                                    kgp = kg - 1
                                    t.wait_ge(p.sem["s"], s_base + kgp + 1)
                                    for i2 in range(2):
                                        ktile = kgp * 2 + i2
                                        r = t.matmul(ps_ctx[0:65, :],
                                                     vtok[:, ktile, 0:65],
                                                     pbuf[:, kgp % 2, i2 * 512:(i2 + 1) * 512],
                                                     start=(kgp == 0 and i2 == 0),
                                                     stop=False)
                                p.inc(r, "t")
                            block.tensor(_sc)

                            def _ex(s, kg=kg, t_base=t_base):
                                psx = ps_sA if kg % 2 == 0 else ps_sB
                                s.wait_ge(p.sem["t"], t_base + kg + 1)
                                r = s.activation(out=pbuf[:, kg % 2, :], in_=psx[:],
                                                 func=AF.Exp)
                                p.inc(r, "s")
                            block.scalar(_ex)

                        def _clast(t, s_base=s_base):
                            t.wait_ge(p.sem["s"], s_base + 32)
                            r = None
                            for i2 in range(2):
                                ktile = 62 + i2
                                r = t.matmul(ps_ctx[0:65, :], vtok[:, ktile, 0:65],
                                             pbuf[:, 1, i2 * 512:(i2 + 1) * 512],
                                             start=False, stop=(i2 == 1))
                            p.inc(r, "t")
                        block.tensor(_clast)

                        def _cev(v):
                            p.wait(v, "t")
                            rr = v.reciprocal(out=rows[64:65, 0:512],
                                              in_=ps_ctx[64:65, :])
                            p.inc(rr, "vv")
                            p.wait(v, "vv")
                            r = v.engine_nop()
                            p.inc(r, "v")
                        block.vector(_cev)

                        def _cbc(t):
                            p.wait(t, "v")
                            r = t.matmul(ps_a[0:64, :], onesff[64:65, 0:64],
                                         rows[64:65, 0:512], start=True, stop=True)
                            p.inc(r, "t")
                        block.tensor(_cbc)

                        def _cfin(v, qt=qt, hp=hp, h01=h01):
                            p.wait(v, "t")
                            v.tensor_copy(out=ctxe[:], in_=ps_ctx[0:64, :])
                            h = hp * 2 + h01
                            r = v.tensor_tensor(
                                out=ctxall[(h % 2) * 64:(h % 2) * 64 + 64,
                                           h // 2, qt * 512:(qt + 1) * 512],
                                in0=ctxe[:], in1=ps_a[0:64, :], op=OP.mult)
                            p.inc(r, "v")
                        block.vector(_cfin)

            def _w2(v):
                r = v.engine_nop()
                p.inc(r, "v")
            block.vector(_w2)
            es2.close()

            es3 = contextlib.ExitStack()
            sb3 = lambda name, shape, dt: es3.enter_context(nc.sbuf_tensor(name, shape, dt))
            wob = sb3("wob", [128, NCT, 1024], BF16)
            slot = sb3("slot", [128, NCT, 2048], BF16)
            sq8 = sb3("sq8", [128, NCT, 512], BF16)
            xrs = sb3("xrs", [128, 2, 2048], F32)
            t1b = sb3("t1b", [128, 512], F32)

            def _ldwo(g):
                p.wait(g, "v")
                for ct in range(NCT):
                    p.inc(g.dma_start(out=wob[:, ct, :],
                                      in_=wo[ct * 128:(ct + 1) * 128, :]),
                          "gld", 16)
            block.gpsimd(_ldwo)

            for tch in range(4):
                sl = slice(tch * 512, (tch + 1) * 512)
                for Dt in range(NCT):
                    pa = ps_a if Dt % 2 == 0 else ps_b

                    def _o(t, pa=pa, sl=sl, Dt=Dt, tch=tch):
                        if tch == 0 and Dt == 0:
                            p.wait(t, "gld")
                        p.wait(t, "v")
                        r = None
                        for ct in range(NCT):
                            r = t.matmul(pa[:], wob[:, ct, Dt * 128:(Dt + 1) * 128],
                                         ctxall[:, ct, sl], start=(ct == 0),
                                         stop=(ct == NCT - 1))
                        p.inc(r, "t")
                    block.tensor(_o)

                    def _e2(v, pa=pa, sl=sl, Dt=Dt):
                        p.wait(v, "t")
                        r = v.scalar_tensor_tensor(out=slot[:, Dt, sl], in0=pa[:],
                                                   scalar=bot[:, Dt:Dt + 1],
                                                   in1=xplain[:, Dt, sl],
                                                   op0=OP.add, op1=OP.add)
                        p.inc(r, "v")
                    block.vector(_e2)

            ln_stats(p, onesb, onesb, lambda ct, sl: slot[:, ct, sl], sq8,
                     ps_r0, ps_r1, rows[0:1, 0:2048], rows[32:33, 0:2048],
                     rows[64:65, 0:2048], T=2048)

            for ct in range(NCT):
                def _lx3(sync, ct=ct):
                    if ct < 2:
                        p.wait(sync, "v")
                    else:
                        p.wait(sync, "st")
                    p.inc(sync.dma_start(out=xrs[:, ct % 2, :],
                                         in_=xt[ct * 128:(ct + 1) * 128, 0:2048]),
                          "ld", 16)
                block.sync(_lx3)

                for tch in range(4):
                    sl = slice(tch * 512, (tch + 1) * 512)

                    def _bc3(t, sl=sl, tch=tch, ct=ct):
                        p.wait(t, "v")
                        if tch == 0:
                            p.wait(t, "ld")
                        t.matmul(ps_a[:], onesff[0:1, :], rows[0:1, sl],
                                 start=True, stop=True)
                        r = t.matmul(ps_b[:], onesff[32:33, :], rows[32:33, sl],
                                     start=True, stop=True)
                        p.inc(r, "t")
                    block.tensor(_bc3)

                    def _ap3(v, sl=sl, ct=ct):
                        p.wait(v, "t")
                        v.scalar_tensor_tensor(out=t1b[:], in0=ps_a[:], scalar=1.0,
                                               in1=slot[:, ct, sl], op0=OP.mult,
                                               op1=OP.add)
                        v.tensor_tensor(out=t1b[:], in0=t1b[:], in1=ps_b[:],
                                        op=OP.mult)
                        r = v.tensor_tensor(out=xrs[:, ct % 2, sl], in0=t1b[:],
                                            in1=xrs[:, ct % 2, sl], op=OP.add)
                        p.inc(r, "v")
                    block.vector(_ap3)

                def _st3(g, ct=ct):
                    p.wait(g, "v")
                    p.inc(g.dma_start(out=xa_out[ct * 128:(ct + 1) * 128, :],
                                      in_=xrs[:, ct % 2, :]), "st", 16)
                block.gpsimd(_st3)

            def _wfin(sync):
                p.wait(sync, "st")
            block.sync(_wfin)
            es3.close()
    return nc


def build_D():
    nc = bass.Bass("TRN2", target_bir_lowering=False, debug=False,
                   detect_race_conditions=False)
    xb = nc.dram_tensor("xb", [D, 2048], F32, kind="ExternalInput").ap()
    od = nc.dram_tensor("od", [D, 2048], F32, kind="ExternalInput").ap()
    owt = nc.dram_tensor("owt", [D, D], F32, kind="ExternalInput").ap()
    fw = nc.dram_tensor("fw", [D, D], F32, kind="ExternalInput").ap()
    xd = nc.dram_tensor("xd", [D, 2048], F32, kind="ExternalOutput").ap()

    es = contextlib.ExitStack()
    sbt = lambda name, shape, dt: es.enter_context(nc.sbuf_tensor(name, shape, dt))
    pst = lambda name, shape: es.enter_context(nc.psum_tensor(name, shape, F32))
    with es:
        xb_f = sbt("xb_f", [128, NCT, 2048], F32)
        obb = sbt("obb", [128, NCT, 2048], BF16)
        owb = sbt("owb", [128, NCT, 1024], BF16)
        fws = sbt("fws", [128, 1, 1024], F32)
        fwb = sbt("fwb", [128, NCT, 1024], BF16)
        slot = sbt("slot", [128, NCT, 2048], BF16)
        sq8 = sbt("sq8", [128, NCT, 512], BF16)
        rows_t = sbt("rows_t", [128, 2048], F32)
        xn4 = obb
        onesb = sbt("onesb", [128, 1], BF16)
        onescf = sbt("onescf", [128, 1], F32)
        onesf = sbt("onesf", [128, 128], F32)
        t1 = sbt("t1", [128, 512], F32)
        negmu, inv, scr_row = rows_t[0:1], rows_t[32:33], rows_t[64:65]
        ps_a = pst("ps_a", [128, 512])
        ps_b = pst("ps_b", [128, 512])
        ps_r0 = pst("ps_r0", [1, 512])
        ps_r1 = pst("ps_r1", [1, 512])

        with nc.Block() as block:
            p = Prog(nc, block, ["ld", "gld", "sld", "t", "v", "s", "st", "vv"], es)

            def _init(v):
                v.memset(onesb[:], 1.0)
                v.memset(onescf[:], 1.0)
                r = v.memset(onesf[:], 1.0)
                p.inc(r, "v")
            block.vector(_init)

            def _lds(sync):
                for ct in range(NCT):
                    p.inc(sync.dma_start(out=xb_f[:, ct, :],
                                         in_=xb[ct * 128:(ct + 1) * 128, :]), "ld", 16)
            block.sync(_lds)

            def _ldg(g):
                for ct in range(NCT):
                    p.inc(g.dma_start(out=obb[:, ct, :],
                                      in_=od[ct * 128:(ct + 1) * 128, :]), "gld", 16)
                for ct in range(NCT):
                    p.inc(g.dma_start(out=owb[:, ct, :],
                                      in_=owt[ct * 128:(ct + 1) * 128, :]), "gld", 16)
            block.gpsimd(_ldg)

            for ct in range(NCT):
                def _lf(sync, ct=ct):
                    if ct >= 1:
                        p.wait(sync, "s")
                    p.inc(sync.dma_start(out=fws[:, 0, :],
                                         in_=fw[ct * 128:(ct + 1) * 128, :]), "ld", 16)
                block.sync(_lf)

                def _sg(s, ct=ct):
                    p.wait(s, "ld")
                    r = s.sign(out=fwb[:, ct, :], in_=fws[:, 0, :])
                    p.inc(r, "s")
                block.scalar(_sg)

            for tch in range(4):
                sl = slice(tch * 512, (tch + 1) * 512)
                for Dt in range(NCT):
                    pa = ps_a if Dt % 2 == 0 else ps_b

                    def _o(t, pa=pa, sl=sl, Dt=Dt, tch=tch):
                        if tch == 0 and Dt == 0:
                            p.wait(t, "gld")
                        p.wait(t, "v")
                        r = None
                        for ct in range(NCT):
                            r = t.matmul(pa[:], owb[:, ct, Dt * 128:(Dt + 1) * 128],
                                         obb[:, ct, sl], start=(ct == 0),
                                         stop=(ct == NCT - 1))
                        p.inc(r, "t")
                    block.tensor(_o)

                    def _e2(v, pa=pa, sl=sl, Dt=Dt):
                        p.wait(v, "t")
                        r = v.tensor_copy(out=slot[:, Dt, sl], in_=pa[:])
                        p.inc(r, "v")
                    block.vector(_e2)

            ln_stats(p, onesb, onesb, lambda ct, sl: slot[:, ct, sl], sq8,
                     ps_r0, ps_r1, negmu, inv, scr_row)
            ln_apply(p, onesf, lambda ct, sl: slot[:, ct, sl], None,
                     negmu, inv, ps_a, ps_b, t1,
                     add_res=lambda ct, sl: xb_f[:, ct, sl],
                     out_res=lambda ct, sl: xb_f[:, ct, sl])

            ln_stats(p, onescf, onesb, lambda ct, sl: xb_f[:, ct, sl], sq8,
                     ps_r0, ps_r1, negmu, inv, scr_row)
            ln_apply(p, onesf, lambda ct, sl: xb_f[:, ct, sl],
                     lambda ct, sl: xn4[:, ct, sl], negmu, inv, ps_a, ps_b, t1)

            for tch in range(4):
                sl = slice(tch * 512, (tch + 1) * 512)
                for Dt in range(NCT):
                    pa = ps_a if Dt % 2 == 0 else ps_b

                    def _f(t, pa=pa, sl=sl, Dt=Dt, tch=tch):
                        if tch == 0 and Dt == 0:
                            p.wait(t, "s")
                        p.wait(t, "v")
                        r = None
                        for ct in range(NCT):
                            r = t.matmul(pa[:], fwb[:, ct, Dt * 128:(Dt + 1) * 128],
                                         xn4[:, ct, sl], start=(ct == 0),
                                         stop=(ct == NCT - 1))
                        p.inc(r, "t")
                    block.tensor(_f)

                    def _e3(v, pa=pa, sl=sl, Dt=Dt):
                        p.wait(v, "t")
                        r = v.tensor_copy(out=slot[:, Dt, sl], in_=pa[:])
                        p.inc(r, "v")
                    block.vector(_e3)

            ln_stats(p, onesb, onesb, lambda ct, sl: slot[:, ct, sl], sq8,
                     ps_r0, ps_r1, negmu, inv, scr_row)
            ln_apply(p, onesf, lambda ct, sl: slot[:, ct, sl], None,
                     negmu, inv, ps_a, ps_b, t1,
                     add_res=lambda ct, sl: xb_f[:, ct, sl],
                     out_res=lambda ct, sl: xb_f[:, ct, sl])

            def _st(sync):
                p.wait(sync, "v")
                for ct in range(NCT):
                    p.inc(sync.dma_start(out=xd[ct * 128:(ct + 1) * 128, :],
                                         in_=xb_f[:, ct, :]), "st", 16)
                p.wait(sync, "st")
            block.sync(_st)
    return nc




T = 8192
PC = 1024
NP = T // PC
NOC = 2
NH = 4


def build_C(dbg=False, xn_only=False, no_w=False):
    nc = bass.Bass("TRN2", target_bir_lowering=False, debug=False,
                   detect_race_conditions=False)
    xb = nc.dram_tensor("xb", [D, T], F32, kind="ExternalInput").ap()
    wr = nc.dram_tensor("wr", [D, 256], F32, kind="ExternalInput").ap()
    wk = nc.dram_tensor("wk", [D, 256], F32, kind="ExternalInput").ap()
    wv = nc.dram_tensor("wv", [D, 256], F32, kind="ExternalInput").ap()
    wg = nc.dram_tensor("wg", [D, 256], F32, kind="ExternalInput").ap()
    las = {pn: nc.dram_tensor(f"la_{pn}", [D, 64], F32, kind="ExternalInput").ap()
           for pn in "rkvgd"}
    lbs = {pn: nc.dram_tensor(f"lb_{pn}", [64, D], F32, kind="ExternalInput").ap()
           for pn in "rkvgd"}
    lbds = nc.dram_tensor("lbds", [64, 256], F32, kind="ExternalInput").ap()
    llv = nc.dram_tensor("llv", [128, 8, 5], F32, kind="ExternalInput").ap()
    llds = nc.dram_tensor("llds", [128, 2], F32, kind="ExternalInput").ap()
    xwv = nc.dram_tensor("xwv", [128, 8], F32, kind="ExternalInput").ap()
    ucol = nc.dram_tensor("ucol", [128, 2], F32, kind="ExternalInput").ap()
    lnwc = nc.dram_tensor("lnwc", [128, 2], F32, kind="ExternalInput").ap()
    s0d = nc.dram_tensor("s0d", [64, 256], F32, kind="ExternalInput").ap()
    hmapd = nc.dram_tensor("hmapd", [128, 2, 8], F32, kind="ExternalInput").ap()
    hseld = nc.dram_tensor("hseld", [8, 2, 128], F32, kind="ExternalInput").ap()
    cmaskd = nc.dram_tensor("cmaskd", [128, 8], F32, kind="ExternalInput").ap()
    xnd = nc.dram_tensor("xnd", [D, T], BF16, kind="Internal").ap()
    od = nc.dram_tensor("od", [256, T], F32, kind="ExternalOutput").ap()
    if dbg:
        dmp = nc.dram_tensor("dmp", [128, 8 * PC], F32, kind="ExternalOutput").ap()
    if xn_only:
        drows = nc.dram_tensor("drows", [128, T], F32, kind="ExternalOutput").ap()
        dmur = nc.dram_tensor("dmur", [128, T], F32, kind="ExternalOutput").ap()
        draw = nc.dram_tensor("draw", [2, T], F32, kind="ExternalOutput").ap()
        divr = nc.dram_tensor("divr", [128, T], F32, kind="ExternalOutput").ap()

    es = contextlib.ExitStack()
    sbt = lambda name, shape, dt: es.enter_context(nc.sbuf_tensor(name, shape, dt))
    pst = lambda name, shape: es.enter_context(nc.psum_tensor(name, shape, F32))
    with es:
        onesb = sbt("onesb", [128, 1], BF16)
        onescf = sbt("onescf", [128, 1], F32)
        onesff = sbt("onesff", [128, 128], F32)
        xwt = sbt("xwt", [128, 8], F32)
        llt = sbt("llt", [128, 8, 5], F32)
        lldt = sbt("lldt", [128, 2], F32)
        uct = sbt("uct", [128, 2], F32)
        lnt = sbt("lnt", [128, 2], F32)
        wpb = sbt("wpb", [128, 8, 4, 256], BF16)
        lab = sbt("lab", [128, 8, 5, 64], BF16)
        lbb = sbt("lbb", [64, 5, 8, 128], BF16)
        lbsb = sbt("lbsb", [64, 256], BF16)
        s0b = sbt("s0b", [64, 256], BF16)
        S = sbt("S", [128, 2, 64], BF16)
        Sg = sbt("Sg", [128, 2, 64], BF16)
        hmap = sbt("hmap", [128, 2, 8], BF16)
        t1 = sbt("t1", [128, 512], F32)
        hsel = sbt("hsel", [8, 2, 128], BF16)
        hself = sbt("hself", [8, 2, 128], F32)
        cmask = sbt("cmask", [128, 8], F32)

        eec = sbt("eec", [128, 2], F32)
        ps_a = pst("ps_a", [128, 512])
        ps_b = pst("ps_b", [128, 512])
        ps_u_full = pst("ps_u", [128, 512])
        ps_u = ps_u_full[:, 0:128]
        ps_y0 = pst("ps_y0", [128, 512])
        ps_y1 = pst("ps_y1", [128, 512])
        ps_r0 = ps_a[0:8, :]
        ps_r1 = ps_b[0:8, :]

        with nc.Block() as block:
            p = Prog(nc, block, ["ld", "gld", "sld", "t", "v", "s", "st",
                                 "vv", "g"], es)

            def _init(v):
                v.memset(onesb[:], 1.0)
                v.memset(onescf[:], 1.0)
                v.memset(onesff[:], 1.0)
                r = v.engine_nop()
                p.inc(r, "v")
            block.vector(_init)

            def _ldm(g):
                p.inc(g.dma_start(out=hmap[:], in_=hmapd[:]), "gld", 16)
                p.inc(g.dma_start(out=hsel[:], in_=hseld[:]), "gld", 16)
                p.inc(g.dma_start(out=hself[:], in_=hseld[:]), "gld", 16)
                p.inc(g.dma_start(out=cmask[:], in_=cmaskd[:]), "gld", 16)
            if not no_w:
                block.gpsimd(_ldm)

            def _ldg(g):
                p.inc(g.dma_start(out=xwt[:], in_=xwv[:]), "gld", 16)
                p.inc(g.dma_start(out=llt[:], in_=llv[:]), "gld", 16)
                p.inc(g.dma_start(out=lldt[:], in_=llds[:]), "gld", 16)
                p.inc(g.dma_start(out=uct[:], in_=ucol[:]), "gld", 16)
                p.inc(g.dma_start(out=lnt[:], in_=lnwc[:]), "gld", 16)
                p.inc(g.dma_start(out=s0b[:], in_=s0d[:]), "gld", 16)
                p.inc(g.dma_start(out=lbsb[:], in_=lbds[:]), "gld", 16)
                for ct in range(NCT):
                    for ip, w in enumerate((wr, wk, wv, wg)):
                        p.inc(g.dma_start(out=wpb[:, ct, ip, :],
                                          in_=w[ct * 128:(ct + 1) * 128, :]), "gld", 16)
                    for ip, pn in enumerate("rkvgd"):
                        p.inc(g.dma_start(out=lab[:, ct, ip, :],
                                          in_=las[pn][ct * 128:(ct + 1) * 128, :]),
                              "gld", 16)
                for ip, pn in enumerate("rkvgd"):
                    p.inc(g.dma_start(
                        out=lbb[:, ip, :, :],
                        in_=lbs[pn].rearrange("l (a p) -> l a p", p=128)), "gld", 16)
            if not no_w:
                block.gpsimd(_ldg)

            def _inits(v):
                p.wait(v, "gld")
                v.tensor_copy(out=Sg[0:64, 0, :], in_=s0b[:, 0:64])
                v.tensor_copy(out=Sg[64:128, 0, :], in_=s0b[:, 64:128])
                v.tensor_copy(out=Sg[0:64, 1, :], in_=s0b[:, 128:192])
                v.tensor_copy(out=Sg[64:128, 1, :], in_=s0b[:, 192:256])
                v.memset(S[:], 0.0)
                r = v.engine_nop()
                p.inc(r, "v")
            if not no_w:
                block.vector(_inits)

            es1 = contextlib.ExitStack()
            sb1 = lambda n_, s_, d_: es1.enter_context(nc.sbuf_tensor(n_, s_, d_))
            xs = sb1("xs", [128, 2, T], F32)
            rows = sb1("rows", [128, T], F32)
            murep = sb1("murep", [128, T], BF16)
            ivrep = sb1("ivrep", [128, T], BF16)
            xnp = sb1("xnp", [128, 2, 512], BF16)
            sqj = sb1("sqj", [128, 2, 512], BF16)

            def _zr(v):
                v.memset(rows[0:1, :], 0.0)
                r = v.memset(rows[32:33, :], 0.0)
                p.inc(r, "v")
            block.vector(_zr)

            for ct in range(NCT):
                def _ld(sync, ct=ct):
                    if ct >= 2:
                        p.wait(sync, "v")
                    p.inc(sync.dma_start(out=xs[:, ct % 2, :],
                                         in_=xb[ct * 128:(ct + 1) * 128, :]),
                          "ld", 16)
                block.sync(_ld)
                for tch in range(16):
                    sl = slice(tch * 512, (tch + 1) * 512)

                    def _sq(v, ct=ct, sl=sl, tch=tch):
                        if tch == 0:
                            p.wait(v, "ld")
                        p.wait(v, "t")
                        r = v.tensor_tensor(out=sqj[:, tch % 2, :],
                                            in0=xs[:, ct % 2, sl],
                                            in1=xs[:, ct % 2, sl], op=OP.mult)
                        p.inc(r, "v")
                    block.vector(_sq)

                    def _mm(t, ct=ct, sl=sl, tch=tch):
                        p.wait(t, "v")
                        t.matmul(ps_a[0:1, :], onescf[:], xs[:, ct % 2, sl],
                                 start=True, stop=True)
                        r = t.matmul(ps_b[0:1, :], onesb[:], sqj[:, tch % 2, :],
                                     start=True, stop=True)
                        p.inc(r, "t")
                    block.tensor(_mm)

                    def _acc(v, sl=sl):
                        p.wait(v, "t")
                        v.tensor_tensor(out=rows[0:1, sl], in0=rows[0:1, sl],
                                        in1=ps_a[0:1, :], op=OP.add)
                        r = v.tensor_tensor(out=rows[32:33, sl], in0=rows[32:33, sl],
                                            in1=ps_b[0:1, :], op=OP.add)
                        p.inc(r, "v")
                    block.vector(_acc)

            if xn_only:
                def _draw(g):
                    p.wait(g, "v")
                    p.inc(g.dma_start(out=draw[0:1, :], in_=rows[0:1, :]), "st", 16)
                    p.inc(g.dma_start(out=draw[1:2, :], in_=rows[32:33, :]), "st", 16)
                block.gpsimd(_draw)

            for tch in range(16):
                sl = slice(tch * 512, (tch + 1) * 512)

                def _fin(v, sl=sl):
                    v.tensor_scalar_mul(out=rows[0:1, sl], in0=rows[0:1, sl],
                                        scalar1=-1.0 / D)
                    v.tensor_scalar_mul(out=rows[32:33, sl], in0=rows[32:33, sl],
                                        scalar1=1.0 / D)
                    v.tensor_tensor(out=ps_a[0:1, :], in0=rows[0:1, sl],
                                    in1=rows[0:1, sl], op=OP.mult)
                    v.scalar_tensor_tensor(out=rows[64:65, sl], in0=ps_a[0:1, :],
                                           scalar=-1.0, in1=rows[32:33, sl],
                                           op0=OP.mult, op1=OP.add)
                    r = v.tensor_scalar_add(out=rows[64:65, sl],
                                            in0=rows[64:65, sl], scalar1=EPS)
                    p.inc(r, "v")
                block.vector(_fin)

            def _sqrt(s):
                p.wait(s, "v")
                r = s.activation(out=rows[64:65, :], in_=rows[64:65, :], func=AF.Sqrt)
                p.inc(r, "s")
            block.scalar(_sqrt)

            def _rcp(v):
                p.wait(v, "s")
                rr = v.reciprocal(out=rows[32:33, :], in_=rows[64:65, :])
                p.inc(rr, "vv")
                p.wait(v, "vv")
                r = v.engine_nop()
                p.inc(r, "v")
            block.vector(_rcp)

            for tch in range(16):
                sl = slice(tch * 512, (tch + 1) * 512)

                def _bc(t, sl=sl):
                    p.wait(t, "v")
                    t.matmul(ps_a[:], onesff[0:1, :], rows[0:1, sl],
                             start=True, stop=True)
                    r = t.matmul(ps_b[:], onesff[32:33, :], rows[32:33, sl],
                                 start=True, stop=True)
                    p.inc(r, "t")
                block.tensor(_bc)

                def _cp(v, sl=sl):
                    p.wait(v, "t")
                    v.tensor_copy(out=murep[:, sl], in_=ps_a[:])
                    r = v.tensor_copy(out=ivrep[:, sl], in_=ps_b[:])
                    p.inc(r, "v")
                block.vector(_cp)

            for ct in range(NCT):
                def _ld2(sync, ct=ct):
                    if ct >= 2:
                        p.wait(sync, "v")
                    else:
                        p.wait(sync, "t")
                    p.inc(sync.dma_start(out=xs[:, ct % 2, :],
                                         in_=xb[ct * 128:(ct + 1) * 128, :]),
                          "ld", 16)
                block.sync(_ld2)
                for tch in range(16):
                    sl = slice(tch * 512, (tch + 1) * 512)

                    def _ap(v, ct=ct, tch=tch, sl=sl):
                        if tch == 0:
                            p.wait(v, "ld")
                        p.wait(v, "g")
                        v.tensor_tensor(out=t1[:], in0=xs[:, ct % 2, sl],
                                        in1=murep[:, sl], op=OP.add)
                        r = v.tensor_tensor(out=xnp[:, tch % 2, :], in0=t1[:],
                                            in1=ivrep[:, sl], op=OP.mult)
                        p.inc(r, "v")
                    block.vector(_ap)

                    def _st1(g, ct=ct, tch=tch, sl=sl):
                        p.wait(g, "v")
                        p.inc(g.dma_start(out=xnd[ct * 128:(ct + 1) * 128, sl],
                                          in_=xnp[:, tch % 2, :]), "g", 16)
                    block.gpsimd(_st1)

            def _w1(v):
                p.wait(v, "g")
                r = v.engine_nop()
                p.inc(r, "v")
            block.vector(_w1)
            if xn_only:
                def _dxn(g):
                    p.wait(g, "v")
                    p.inc(g.dma_start(out=drows[0:1, :], in_=rows[0:1, :]), "st", 16)
                    p.inc(g.dma_start(out=drows[32:33, :], in_=rows[32:33, :]), "st", 16)
                    p.inc(g.dma_start(out=dmur[:], in_=murep[:]), "st", 16)
                    p.inc(g.dma_start(out=divr[:], in_=ivrep[:]), "st", 16)
                block.gpsimd(_dxn)
                def _wf0(sync):
                    p.wait(sync, "st")
                block.sync(_wf0)
                es1.close()
                es2 = contextlib.ExitStack()
                return nc
            es1.close()

            es2 = contextlib.ExitStack()
            sb2 = lambda n_, s_, d_: es2.enter_context(nc.sbuf_tensor(n_, s_, d_))
            xnh = sb2("xnh", [128, NCT, PC + 1], BF16)
            dx = sb2("dxб" if False else "dxb", [128, NCT, PC], BF16)
            lerpx = sb2("lerpx", [128, NCT, PC], BF16)
            mp = sb2("mp", [128, NCT, PC], BF16)
            tanA = sb2("tanA", [64, PC], BF16)
            rc = sb2("rc", [128, NOC, PC], BF16)
            kc = sb2("kc", [128, NOC, PC], BF16)
            vc = sb2("vc", [128, NOC, PC], BF16)
            gc = sb2("gc", [128, NOC, PC], BF16)
            zc = sb2("zc", [128, NOC, PC], BF16)
            Pf = sb2("Pf", [128, NOC, PC + 1], F32)
            ezf = sb2("ezf", [128, NOC, PC], F32)
            onespc = sb2("onespc", [128, PC], F32)
            wdec = sb2("wdec", [128, NOC, PC], BF16)
            kh = sb2("kh", [128, NOC, PC], BF16)
            rt = sb2("rt", [128, NOC, PC], BF16)
            rco = sb2("rco", [128, NOC, PC], BF16)
            Et = sb2("Et", [128, NOC, PC], BF16)
            prodb = sb2("prodb", [128, NOC, PC], BF16)
            yT = sb2("yT", [128, NOC, PC], BF16)
            scoreb = sb2("scoreb", [8, PC], BF16)
            ktok = sb2("ktok", [128, PC // 128, 256], BF16)
            vtok = sb2("vtok", [128, PC // 128, 256], BF16)
            km8 = sb2("km8", [128, 8, 256], BF16)
            murow = sb2("murow", [8, 512], F32)
            ivrow = sb2("ivrow", [8, 512], F32)
            osc = sb2("osc", [128, 2, 512], F32)
            dec = sb2("dec", [128, 2, PC // 8], F32)


            for pc in range(NP):
                off = pc * PC

                def _lxn(sync, pc=pc, off=off):
                    p.wait(sync, "v")
                    if pc == 0:
                        p.wait(sync, "g")
                    for ct in range(NCT):
                        if pc == 0:
                            p.inc(sync.dma_start(out=xnh[:, ct, 1:PC + 1],
                                                 in_=xnd[ct * 128:(ct + 1) * 128,
                                                         0:PC]), "ld", 16)
                        else:
                            p.inc(sync.dma_start(out=xnh[:, ct, :],
                                                 in_=xnd[ct * 128:(ct + 1) * 128,
                                                         off - 1:off + PC]),
                                  "ld", 16)
                block.sync(_lxn)

                if pc == 0:
                    def _z0(v):
                        r = v.memset(xnh[:, :, 0:1], 0.0)
                        p.inc(r, "v")
                    block.vector(_z0)

                def _dx(v, pc=pc):
                    p.wait(v, "ld")
                    r = None
                    for ct in range(NCT):
                        v.tensor_tensor(out=dx[:, ct, :], in0=xnh[:, ct, 0:PC],
                                        in1=xnh[:, ct, 1:PC + 1], op=OP.subtract)
                        r = v.scalar_tensor_tensor(out=lerpx[:, ct, :],
                                                   in0=dx[:, ct, :],
                                                   scalar=xwt[:, ct:ct + 1],
                                                   in1=xnh[:, ct, 1:PC + 1],
                                                   op0=OP.mult, op1=OP.add)
                    p.inc(r, "v")
                block.vector(_dx)

                def emit_lora(ip, src_tiles, dst, dst_oc, add_m, llcol, lbsel):
                    for nchunk in range(PC // 512):
                        nsl = slice(nchunk * 512, (nchunk + 1) * 512)
                        pu = ps_a if nchunk % 2 == 0 else ps_b

                        def _la(t, ip=ip, nsl=nsl, pu=pu):
                            p.wait(t, "v")
                            if ip == 0 and nchunk == 0:
                                p.wait(t, "gld")
                            r = None
                            for ct in range(NCT):
                                r = t.matmul(pu[0:64, :], lab[:, ct, ip, :],
                                             src_tiles(ct, nsl),
                                             start=(ct == 0), stop=(ct == NCT - 1))
                            p.inc(r, "t")
                        block.tensor(_la)

                        def _th(s, nsl=nsl, pu=pu):
                            p.wait(s, "t")
                            r = s.activation(out=tanA[:, nsl], in_=pu[0:64, :],
                                             func=AF.Tanh)
                            p.inc(r, "s")
                        block.scalar(_th)
                    nct_out = NCT if not dst_oc else NOC
                    for nchunk in range(PC // 512):
                        nsl = slice(nchunk * 512, (nchunk + 1) * 512)
                        for ct in range(nct_out):
                            pu = ps_a if (nchunk * nct_out + ct) % 2 == 0 else ps_b

                            def _lb(t, ip=ip, nsl=nsl, ct=ct, pu=pu, lbsel=lbsel):
                                p.wait(t, "s")
                                p.wait(t, "v")
                                lhs = (lbsb[:, ct * 128:(ct + 1) * 128] if lbsel
                                       else lbb[:, ip, ct, :])
                                r = t.matmul(pu[:], lhs, tanA[:, nsl],
                                             start=True, stop=True)
                                p.inc(r, "t")
                            block.tensor(_lb)

                            def _ep(v, ip=ip, nsl=nsl, ct=ct, pu=pu,
                                    add_m=add_m, llcol=llcol, dst=dst):
                                p.wait(v, "t")
                                if add_m:
                                    v.scalar_tensor_tensor(out=t1[:], in0=pu[:],
                                                           scalar=llcol(ct),
                                                           in1=dx[:, ct, nsl],
                                                           op0=OP.add, op1=OP.mult)
                                    r = v.tensor_tensor(out=dst[:, ct, nsl],
                                                        in0=t1[:],
                                                        in1=xnh[:, ct, nsl.start + 1:nsl.stop + 1],
                                                        op=OP.add)
                                else:
                                    r = v.tensor_scalar(out=dst[:, ct, nsl],
                                                        in0=pu[:], scalar1=llcol(ct),
                                                        scalar2=None, op0=OP.add)
                                p.inc(r, "v")
                            block.vector(_ep)

                for ip, (pn, dstc) in enumerate(zip("rkvg", (rc, kc, vc, gc))):
                    emit_lora(ip, lambda ct, nsl: lerpx[:, ct, nsl], mp, False,
                              True, lambda ct, ip=ip: llt[:, ct, ip:ip + 1],
                              False)
                    if dbg and pc == NP - 1 and ip == 0:
                        def _dmp(g):
                            p.wait(g, "v")
                            p.inc(g.dma_start(out=dmp[:],
                                              in_=mp[:].rearrange("p a b -> p (a b)")),
                                  "st", 16)
                        block.gpsimd(_dmp)
                    for nchunk in range(PC // 512):
                        nsl = slice(nchunk * 512, (nchunk + 1) * 512)
                        for oc in range(NOC):
                            pu = ps_a if (nchunk * NOC + oc) % 2 == 0 else ps_b

                            def _pj(t, ip=ip, nsl=nsl, oc=oc, pu=pu):
                                p.wait(t, "v")
                                r = None
                                for ct in range(NCT):
                                    r = t.matmul(pu[:], wpb[:, ct, ip,
                                                          oc * 128:(oc + 1) * 128],
                                                 mp[:, ct, nsl],
                                                 start=(ct == 0), stop=(ct == NCT - 1))
                                p.inc(r, "t")
                            block.tensor(_pj)

                            def _pe(v, nsl=nsl, oc=oc, pu=pu, dstc=dstc):
                                p.wait(v, "t")
                                r = v.tensor_copy(out=dstc[:, oc, nsl], in_=pu[:])
                                p.inc(r, "v")
                            block.vector(_pe)

                emit_lora(4, lambda ct, nsl: lerpx[:, ct, nsl], mp, False, True,
                          lambda ct: llt[:, ct, 4:5], False)
                emit_lora(4, lambda ct, nsl: mp[:, ct, nsl], zc, True, False,
                          lambda ct: lldt[:, ct:ct + 1], True)

                def _ez(s, pc=pc):
                    p.wait(s, "v")
                    r = None
                    for oc in range(NOC):
                        s.activation(out=ezf[:, oc, :], in_=zc[:, oc, :], func=AF.Exp)
                        r = s.activation(out=wdec[:, oc, :], in_=ezf[:, oc, :],
                                         func=AF.Exp, scale=-1.0)
                    p.inc(r, "s")
                block.scalar(_ez)

                def _ps(v, pc=pc):
                    p.wait(v, "s")
                    if pc == 0:
                        v.memset(onespc[:], 1.0)
                    v.memset(Pf[:, :, 0:1], 0.0)
                    r = None
                    for oc in range(NOC):
                        r = v.tensor_tensor_scan(out=Pf[:, oc, 1:PC + 1],
                                                 data0=onespc[:],
                                                 data1=ezf[:, oc, :],
                                                 initial=0.0, op0=OP.mult,
                                                 op1=OP.add)
                    p.inc(r, "vv")
                    p.wait(v, "vv")
                    r = v.engine_nop()
                    p.inc(r, "v")
                block.vector(_ps)

                for oc in range(NOC):
                    for j in range(8):
                        def _s1(v, oc=oc, j=j):
                            p.wait(v, "s")
                            r = v.tensor_tensor(out=osc[:, 0, 0:PC // 8],
                                                in0=Pf[:, oc, j + 1:PC + 1:8],
                                                in1=Pf[:, oc, 8:PC + 1:8],
                                                op=OP.subtract)
                            p.inc(r, "v")
                        block.vector(_s1)

                        def _s2(s, oc=oc, j=j):
                            p.wait(s, "v")
                            r = s.activation(out=osc[:, 1, 0:PC // 8], in_=osc[:, 0, 0:PC // 8],
                                             func=AF.Exp)
                            p.inc(r, "s")
                        block.scalar(_s2)

                        def _s3(v, oc=oc, j=j):
                            p.wait(v, "s")
                            r = v.tensor_tensor(out=kh[:, oc, j::8],
                                                in0=osc[:, 1, 0:PC // 8],
                                                in1=kc[:, oc, j::8], op=OP.mult)
                            p.inc(r, "v")
                        block.vector(_s3)

                        def _s4(v, oc=oc, j=j):
                            p.wait(v, "s")
                            r = v.tensor_tensor(out=osc[:, 0, 0:PC // 8],
                                                in0=Pf[:, oc, 0:PC:8],
                                                in1=Pf[:, oc, j:PC:8],
                                                op=OP.subtract)
                            p.inc(r, "v")
                        block.vector(_s4)

                        def _s5(s, oc=oc, j=j):
                            p.wait(s, "v")
                            r = s.activation(out=osc[:, 1, 0:PC // 8], in_=osc[:, 0, 0:PC // 8],
                                             func=AF.Exp)
                            p.inc(r, "s")
                        block.scalar(_s5)

                        def _s6(v, oc=oc, j=j):
                            p.wait(v, "s")
                            r = v.tensor_tensor(out=rt[:, oc, j::8],
                                                in0=osc[:, 1, 0:PC // 8],
                                                in1=rc[:, oc, j::8], op=OP.mult)
                            p.inc(r, "v")
                        block.vector(_s6)

                    def _s7(s, oc=oc):
                        p.wait(s, "v")
                        r = s.activation(out=mp[:, 0, 0:PC],
                                         in_=Pf[:, oc, 0:PC], func=AF.Exp, scale=-1.0)
                        p.inc(r, "s")
                    block.scalar(_s7)

                    def _s8(v, oc=oc):
                        p.wait(v, "s")
                        r = v.tensor_tensor(out=rco[:, oc, :], in0=mp[:, 0, 0:PC],
                                            in1=rc[:, oc, :], op=OP.mult)
                        p.inc(r, "v")
                    block.vector(_s8)

                    def _s9(v, oc=oc):
                        p.wait(v, "s")
                        r = v.tensor_tensor(out=osc[:, 0, 0:PC // 8],
                                            in0=Pf[:, oc, 0:PC:8],
                                            in1=Pf[:, oc, 8:PC + 1:8],
                                            op=OP.subtract)
                        p.inc(r, "v")
                    block.vector(_s9)

                    def _s10(s, oc=oc):
                        p.wait(s, "v")
                        r = s.activation(out=dec[:, oc, :], in_=osc[:, 0, 0:PC // 8],
                                         func=AF.Exp)
                        p.inc(r, "s")
                    block.scalar(_s10)

                def _wq(v):
                    p.wait(v, "s")
                    r = v.engine_nop()
                    p.inc(r, "v")
                block.vector(_wq)

                def _einit(v):
                    r = None
                    for oc in range(NOC):
                        r = v.tensor_copy(out=Et[:, oc, :], in_=rc[:, oc, :])
                        r = v.memset(yT[:, oc, :], 0.0)
                    p.inc(r, "v")
                block.vector(_einit)

                for dlt in range(0, 8):
                    def _lag1(v, dlt=dlt):
                        r = None
                        if dlt == 0:
                            for oc in range(NOC):
                                r = v.scalar_tensor_tensor(out=prodb[:, oc, :],
                                                           in0=rc[:, oc, :],
                                                           scalar=uct[:, oc:oc + 1],
                                                           in1=kc[:, oc, :],
                                                           op0=OP.mult, op1=OP.mult)
                        else:
                            for oc in range(NOC):
                                if dlt >= 2:
                                    v.tensor_tensor(out=Et[:, oc, dlt - 1:],
                                                    in0=Et[:, oc, dlt - 1:],
                                                    in1=wdec[:, oc, 0:PC - dlt + 1],
                                                    op=OP.mult)
                                r = v.tensor_tensor(out=prodb[:, oc, dlt:],
                                                    in0=Et[:, oc, dlt:],
                                                    in1=kc[:, oc, 0:PC - dlt],
                                                    op=OP.mult)
                        p.inc(r, "v")
                    block.vector(_lag1)

                    for nchunk in range(PC // 512):
                        nsl = slice(nchunk * 512, (nchunk + 1) * 512)

                        def _lag2(t, nsl=nsl, dlt=dlt, nchunk=nchunk):
                            p.wait(t, "v")
                            if nchunk > 0 or dlt > 0:
                                pass
                            for oc in range(NOC):
                                r = t.matmul(ps_a[0:8, :], hmap[:, oc, :],
                                             prodb[:, oc, nsl],
                                             start=(oc == 0), stop=(oc == 1))
                            p.inc(r, "t")
                        block.tensor(_lag2)

                        def _lag3(v, nsl=nsl, dlt=dlt):
                            p.wait(v, "t")
                            r = v.tensor_copy(out=scoreb[:, nsl], in_=ps_a[0:8, :])
                            p.inc(r, "v")
                        block.vector(_lag3)

                    if dlt > 0:
                        def _lagm(v, dlt=dlt):
                            r = None
                            for j in range(dlt):
                                r = v.memset(scoreb[:, j::8], 0.0)
                            p.inc(r, "v")
                        block.vector(_lagm)

                    for nchunk in range(PC // 512):
                        nsl = slice(nchunk * 512, (nchunk + 1) * 512)

                        def _lag4(t, nsl=nsl):
                            p.wait(t, "v")
                            r = None
                            for oc in range(NOC):
                                pu = ps_y0 if oc == 0 else ps_y1
                                r = t.matmul(pu[:], hsel[:, oc, :], scoreb[:, nsl],
                                             start=True, stop=True)
                            p.inc(r, "t")
                        block.tensor(_lag4)

                        def _lag5(v, nsl=nsl, dlt=dlt):
                            p.wait(v, "t")
                            r = None
                            for oc in range(NOC):
                                pu = ps_y0 if oc == 0 else ps_y1
                                if dlt == 0:
                                    v.scalar_tensor_tensor(out=t1[:], in0=pu[:],
                                                           scalar=1.0,
                                                           in1=vc[:, oc, nsl],
                                                           op0=OP.mult, op1=OP.mult)
                                else:
                                    vsh = vc[:, oc, nsl.start - dlt:nsl.stop - dlt] \
                                        if nsl.start >= dlt else None
                                    if vsh is None:
                                        v.scalar_tensor_tensor(
                                            out=t1[:, dlt:512], in0=pu[:, dlt:512],
                                            scalar=1.0,
                                            in1=vc[:, oc, nsl.start:nsl.stop - dlt],
                                            op0=OP.mult, op1=OP.mult)
                                        v.memset(t1[:, 0:dlt], 0.0)
                                    else:
                                        v.scalar_tensor_tensor(out=t1[:], in0=pu[:],
                                                               scalar=1.0, in1=vsh,
                                                               op0=OP.mult,
                                                               op1=OP.mult)
                                r = v.tensor_tensor(out=yT[:, oc, nsl],
                                                    in0=yT[:, oc, nsl],
                                                    in1=t1[:], op=OP.add)
                            p.inc(r, "v")
                        block.vector(_lag5)

                def _tr(s):
                    p.wait(s, "v")
                    for tt in range(PC // 128):
                        for oc in range(NOC):
                            p.inc(s.dma_start(out=ktok[:, tt, oc * 128:(oc + 1) * 128],
                                              in_=kh[:, oc, tt * 128:(tt + 1) * 128],
                                              transpose=True), "sld", 16)
                            p.inc(s.dma_start(out=vtok[:, tt, oc * 128:(oc + 1) * 128],
                                              in_=vc[:, oc, tt * 128:(tt + 1) * 128],
                                              transpose=True), "sld", 16)
                block.scalar(_tr)

                for reg in range(PC // 512):
                    rsl = slice(reg * 512, (reg + 1) * 512)
                    def _corr(t, reg=reg, rsl=rsl, pc=pc):
                        p.wait(t, "v")
                        if reg == 0:
                            p.wait(t, "sld")
                        r = None
                        for h in range(NH):
                            oc, hb = h // 2, (h % 2) * 64
                            pu = ps_y0 if oc == 0 else ps_y1
                            r = t.matmul(pu[hb:hb + 64, :],
                                         Sg[hb:hb + 64, oc, :],
                                         rco[hb:hb + 64, oc, rsl],
                                         start=True, stop=False,
                                         skip_group_check=True)
                        p.inc(r, "t")
                    block.tensor(_corr)

                    for tt4 in range(4):
                        tt = reg * 4 + tt4

                        def _msk(v, tt=tt):
                            p.wait(v, "t")
                            r = None
                            for m in range(8):
                                r = v.tensor_scalar(out=km8[:, m, :],
                                                    in0=ktok[:, tt, :],
                                                    scalar1=cmask[:, m:m + 1],
                                                    scalar2=None, op0=OP.mult)
                            p.inc(r, "v")
                        block.vector(_msk)

                        for ck in range(16):
                            c_abs = tt * 16 + ck
                            b64 = (ck // 8) * 64
                            m = ck % 8
                            csl = slice(tt4 * 128 + ck * 8, tt4 * 128 + ck * 8 + 8)

                            def _umm(t, tt=tt, ck=ck, b64=b64, m=m, csl=csl):
                                p.wait(t, "v")
                                r = None
                                last = (ck == 15 and (tt % 4) == 3)
                                for h in range(NH):
                                    oc, hb = h // 2, (h % 2) * 64
                                    pu = ps_y0 if oc == 0 else ps_y1
                                    t.matmul(pu[hb:hb + 64, csl],
                                             S[hb:hb + 64, oc, :],
                                             rt[hb:hb + 64, oc,
                                                tt * 128 + (csl.start % 128):
                                                tt * 128 + (csl.start % 128) + 8],
                                             start=False, stop=last,
                                             skip_group_check=True)
                                for h in range(NH):
                                    oc, hb = h // 2, (h % 2) * 64
                                    r = t.matmul(ps_u[hb:hb + 64,
                                                      oc * 64:(oc + 1) * 64],
                                                 km8[b64:b64 + 64, m,
                                                     oc * 128 + hb:oc * 128 + hb + 64],
                                                 vtok[b64:b64 + 64, tt,
                                                      oc * 128 + hb:oc * 128 + hb + 64],
                                                 start=True, stop=True)
                                p.inc(r, "t")
                            block.tensor(_umm)

                            def _sup(v, c_abs=c_abs):
                                p.wait(v, "t")
                                r = None
                                for oc in range(NOC):
                                    r = v.scalar_tensor_tensor(
                                        out=S[:, oc, :], in0=S[:, oc, :],
                                        scalar=dec[:, oc, c_abs:c_abs + 1],
                                        in1=ps_u[:, oc * 64:(oc + 1) * 64],
                                        op0=OP.mult, op1=OP.add)
                                p.inc(r, "v")
                            block.vector(_sup)

                    def _mrg(v, rsl=rsl, reg=reg):
                        p.wait(v, "t")
                        r = None
                        for oc in range(NOC):
                            pu = ps_y0 if oc == 0 else ps_y1
                            r = v.tensor_tensor(out=yT[:, oc, rsl],
                                                in0=yT[:, oc, rsl], in1=pu[:],
                                                op=OP.add)
                        p.inc(r, "v")
                    block.vector(_mrg)

                def _sgu(s, pc=pc):
                    p.wait(s, "v")
                    r = None
                    for oc in range(NOC):
                        r = s.activation(out=eec[:, oc:oc + 1],
                                         in_=Pf[:, oc, PC:PC + 1], func=AF.Exp,
                                         scale=-1.0)
                    p.inc(r, "s")
                block.scalar(_sgu)

                def _sgu2(v, pc=pc):
                    p.wait(v, "s")
                    r = None
                    for oc in range(NOC):
                        r = v.tensor_scalar(out=Sg[:, oc, :], in0=Sg[:, oc, :],
                                            scalar1=eec[:, oc:oc + 1], scalar2=None,
                                            op0=OP.mult)
                    p.inc(r, "v")
                block.vector(_sgu2)

                for nchunk in range(PC // 512):
                    nsl = slice(nchunk * 512, (nchunk + 1) * 512)

                    def _ys(v, nsl=nsl):
                        p.wait(v, "t")
                        r = None
                        for oc in range(NOC):
                            r = v.tensor_tensor(out=prodb[:, oc, nsl],
                                                in0=yT[:, oc, nsl],
                                                in1=yT[:, oc, nsl], op=OP.mult)
                        p.inc(r, "v")
                    block.vector(_ys)

                    def _ym(t, nsl=nsl):
                        p.wait(t, "v")
                        for oc in range(NOC):
                            t.matmul(ps_a[0:8, :], hmap[:, oc, :], yT[:, oc, nsl],
                                     start=(oc == 0), stop=(oc == 1))
                        for oc in range(NOC):
                            r = t.matmul(ps_b[0:8, :], hmap[:, oc, :],
                                         prodb[:, oc, nsl],
                                         start=(oc == 0), stop=(oc == 1))
                        p.inc(r, "t")
                    block.tensor(_ym)

                    def _yr(v, nsl=nsl):
                        p.wait(v, "t")
                        v.tensor_scalar_mul(out=murow[:], in0=ps_a[0:8, :],
                                            scalar1=1.0 / 64.0)
                        v.tensor_tensor(out=ivrow[:], in0=murow[:], in1=murow[:],
                                        op=OP.mult)
                        v.scalar_tensor_tensor(out=ivrow[:], in0=ps_b[0:8, :],
                                               scalar=1.0 / 64.0, in1=ivrow[:],
                                               op0=OP.mult, op1=OP.subtract)
                        rr = v.tensor_scalar_add(out=ivrow[:], in0=ivrow[:],
                                                 scalar1=EPS)
                        p.inc(rr, "vv")
                        p.wait(v, "vv")
                        r = v.engine_nop()
                        p.inc(r, "v")
                    block.vector(_yr)

                    def _ysq(s, nsl=nsl):
                        p.wait(s, "v")
                        r = s.activation(out=ivrow[:], in_=ivrow[:], func=AF.Sqrt)
                        p.inc(r, "s")
                    block.scalar(_ysq)

                    def _yrc(v, nsl=nsl):
                        p.wait(v, "s")
                        rr = v.reciprocal(out=ivrow[:], in_=ivrow[:])
                        p.inc(rr, "vv")
                        p.wait(v, "vv")
                        v.tensor_scalar_mul(out=murow[:], in0=murow[:], scalar1=-1.0)
                        r = v.engine_nop()
                        p.inc(r, "v")
                    block.vector(_yrc)

                    def _yb(t, nsl=nsl):
                        p.wait(t, "v")
                        r = None
                        for oc in range(NOC):
                            pu = ps_y0 if oc == 0 else ps_y1
                            t.matmul(pu[:], hself[:, oc, :], murow[:],
                                     start=True, stop=True)
                            pu2 = ps_a if oc == 0 else ps_b
                            r = t.matmul(pu2[:], hself[:, oc, :], ivrow[:],
                                         start=True, stop=True)
                        p.inc(r, "t")
                    block.tensor(_yb)

                    def _yap(v, nsl=nsl):
                        p.wait(v, "t")
                        r = None
                        for oc in range(NOC):
                            pu = ps_y0 if oc == 0 else ps_y1
                            pu2 = ps_a if oc == 0 else ps_b
                            v.tensor_tensor(out=t1[:], in0=yT[:, oc, nsl], in1=pu[:],
                                            op=OP.add)
                            v.tensor_tensor(out=t1[:], in0=t1[:], in1=pu2[:],
                                            op=OP.mult)
                            r = v.tensor_scalar(out=osc[:, oc, :], in0=t1[:],
                                                scalar1=lnt[:, 0:1],
                                                scalar2=lnt[:, 1:2],
                                                op0=OP.mult, op1=OP.add)
                        p.inc(r, "v")
                    block.vector(_yap)

                    def _sil(s, nsl=nsl):
                        p.wait(s, "v")
                        r = None
                        for oc in range(NOC):
                            r = s.activation(out=t1[:] if False else prodb[:, oc, nsl],
                                             in_=gc[:, oc, nsl], func=AF.Silu)
                        p.inc(r, "s")
                    block.scalar(_sil)

                    def _oo(v, nsl=nsl, off=off):
                        p.wait(v, "s")
                        r = None
                        for oc in range(NOC):
                            r = v.tensor_tensor(out=osc[:, oc, :],
                                                in0=osc[:, oc, :],
                                                in1=prodb[:, oc, nsl], op=OP.mult)
                        p.inc(r, "v")
                    block.vector(_oo)

                    def _ost(g, nsl=nsl, off=off):
                        p.wait(g, "v")
                        for oc in range(NOC):
                            p.inc(g.dma_start(out=od[oc * 128:(oc + 1) * 128,
                                                     off + nsl.start:off + nsl.stop],
                                              in_=osc[:, oc, :]), "st", 16)
                    block.gpsimd(_ost)

            def _wfin(sync):
                p.wait(sync, "st")
            block.sync(_wfin)
            es2.close()
    return nc


def kernel(**inp):
    x0 = np.asarray(inp["L0_attn_hidden_states"], np.float32)

    Wq = np.asarray(inp["L0_attn_W_qkv"], np.float32)
    cols = []
    for h in range(16):
        cols.extend(range(h * 64, h * 64 + 64))
        cols.extend(range(1024 + h * 64, 1024 + h * 64 + 64))
        cols.extend(range(2048 + h * 64, 2048 + h * 64 + 64))
    wq_t = np.ascontiguousarray(Wq[cols, :].T)
    bq_t = np.ascontiguousarray(np.asarray(inp["L0_attn_b_qkv"], np.float32)[cols].reshape(24, 128).T)
    gm_t = np.ascontiguousarray(np.asarray(inp["L0_attn_ln_gamma"], np.float32).reshape(8, 128).T)
    bt_t = np.ascontiguousarray(np.asarray(inp["L0_attn_ln_beta"], np.float32).reshape(8, 128).T)
    wo_t = np.ascontiguousarray(np.asarray(inp["L0_attn_W_out"], np.float32).T)
    bo_t = np.ascontiguousarray(np.asarray(inp["L0_attn_b_out"], np.float32).reshape(8, 128).T)
    ncA = build_A()
    in_maps = []
    for cid in range(8):
        b, q = cid // 4, cid % 4
        xr = np.roll(x0[b], -q * 2048, axis=0)
        in_maps.append({"xt": np.ascontiguousarray(xr.T), "wq": wq_t, "bq": bq_t,
                        "gm": gm_t, "bt": bt_t, "wo": wo_t, "bo": bo_t})
    resA = run_bass_kernel_spmd(ncA, in_maps, core_ids=list(range(8)))
    xa_sl = [np.asarray(resA.results[cid]["xa_out"], np.float32) for cid in range(8)]

    gw_t = np.ascontiguousarray(np.asarray(inp["L0_ffn_gate_W"], np.float32).T)
    e0_t = np.ascontiguousarray(np.asarray(inp["L0_ffn_expert_W"], np.float32)[0].T)
    e1_t = np.ascontiguousarray(np.asarray(inp["L0_ffn_expert_W"], np.float32)[1].T)
    ow_t = np.ascontiguousarray(np.asarray(inp["L0_ffn_out_W"], np.float32).T)
    ncB = build_B()
    in_maps = [{"xa": np.ascontiguousarray(xa_sl[cid]), "gw": gw_t, "e0": e0_t,
                "e1": e1_t, "ow": ow_t} for cid in range(8)]
    resB = run_bass_kernel_spmd(ncB, in_maps, core_ids=list(range(8)))
    xb_sl = [np.asarray(resB.results[cid]["xb"], np.float32) for cid in range(8)]
    xb_full = [np.ascontiguousarray(np.concatenate([xb_sl[b * 4 + q] for q in range(4)], axis=1))
               for b in range(2)]

    hmapd = np.zeros((128, 2, 8), np.float32)
    hseld = np.zeros((8, 2, 128), np.float32)
    cmaskd = np.zeros((128, 8), np.float32)
    for hl in range(4):
        oc, hb = hl // 2, (hl % 2) * 64
        hmapd[hb:hb + 64, oc, hl] = 1.0
        hseld[hl, oc, hb:hb + 64] = 1.0
    for p_ in range(128):
        cmaskd[p_, (p_ % 64) // 8] = 1.0
    ncC = build_C()
    in_maps = []
    for cid in range(8):
        b, hq = cid // 4, cid % 4
        chs = slice(hq * 256, (hq + 1) * 256)
        f = {"xb": xb_full[b],
             "wr": np.ascontiguousarray(np.asarray(inp["L1_attn_rW"], np.float32)[chs, :].T),
             "wk": np.ascontiguousarray(np.asarray(inp["L1_attn_kW"], np.float32)[chs, :].T),
             "wv": np.ascontiguousarray(np.asarray(inp["L1_attn_vW"], np.float32)[chs, :].T),
             "wg": np.ascontiguousarray(np.asarray(inp["L1_attn_gW"], np.float32)[chs, :].T),
             "lbds": np.ascontiguousarray(np.asarray(inp["L1_attn_d_lb"], np.float32).T[:, chs]),
             "llds": np.ascontiguousarray(np.asarray(inp["L1_attn_d_ll"], np.float32)[chs].reshape(2, 128).T),
             "xwv": np.ascontiguousarray(np.asarray(inp["L1_attn_xw"], np.float32).reshape(8, 128).T),
             "ucol": np.ascontiguousarray(np.asarray(inp["L1_attn_u"], np.float32)[chs].reshape(2, 128).T),
             "hmapd": hmapd, "hseld": hseld, "cmaskd": cmaskd}
        for pn in "rkvgd":
            f[f"la_{pn}"] = np.ascontiguousarray(np.asarray(inp[f"L1_attn_{pn}_la"], np.float32).T)
            f[f"lb_{pn}"] = np.ascontiguousarray(np.asarray(inp[f"L1_attn_{pn}_lb"], np.float32).T)
        llv = np.zeros((128, 8, 5), np.float32)
        for ipn, pn in enumerate("rkvgd"):
            llv[:, :, ipn] = np.asarray(inp[f"L1_attn_{pn}_ll"], np.float32).reshape(8, 128).T
        f["llv"] = llv
        lnwc = np.zeros((128, 2), np.float32)
        lnwc[:, 0] = np.tile(np.asarray(inp["L1_attn_ln_w"], np.float32), 2)
        lnwc[:, 1] = np.tile(np.asarray(inp["L1_attn_ln_b"], np.float32), 2)
        f["lnwc"] = lnwc
        s0d = np.zeros((64, 256), np.float32)
        for hl in range(4):
            s0d[:, hl * 64:(hl + 1) * 64] = np.asarray(inp["L1_attn_init_state"], np.float32)[hq * 4 + hl]
        f["s0d"] = s0d
        in_maps.append(f)
    resC = run_bass_kernel_spmd(ncC, in_maps, core_ids=list(range(8)))
    o_full = [np.ascontiguousarray(np.concatenate(
        [np.asarray(resC.results[b * 4 + hq]["od"], np.float32) for hq in range(4)], axis=0))
        for b in range(2)]

    owt = np.ascontiguousarray(np.asarray(inp["L1_attn_oW"], np.float32).T)
    fwt = np.ascontiguousarray(np.asarray(inp["L1_ffn_weight"], np.float32).T)
    ncD = build_D()
    in_maps = []
    for cid in range(8):
        b, q = cid // 4, cid % 4
        in_maps.append({"xb": np.ascontiguousarray(xb_sl[cid]),
                        "od": np.ascontiguousarray(o_full[b][:, q * 2048:(q + 1) * 2048]),
                        "owt": owt, "fw": fwt})
    resD = run_bass_kernel_spmd(ncD, in_maps, core_ids=list(range(8)))
    out = np.zeros((2, 8192, 1024), np.float32)
    for cid in range(8):
        b, q = cid // 4, cid % 4
        out[b, q * 2048:(q + 1) * 2048, :] = np.asarray(resD.results[cid]["xd"], np.float32).T
    return out
